# version 01
# baseline (speedup 1.0000x reference)
import math
import numpy as np
import ml_dtypes

import concourse.bass as bass
import concourse.mybir as mybir
from concourse.bass_utils import run_bass_kernel_spmd

F32 = mybir.dt.float32
BF16 = mybir.dt.bfloat16
AF = mybir.ActivationFunctionType
ALU = mybir.AluOpType
AX = mybir.AxisListType

NCORES = 8
B_PER_CORE = 4
T = 2048
D = 1024
NCH = 8
TT = 512
NTT = T // TT
GW = T + 1
NEG = -1.0e6

ENGINES = ("sync", "tensor", "vector", "scalar", "gpsimd")
DEBUG_MOE = None
DEBUG_ATTN = 0
DEBUG_ATTN_TT = 0
DEBUG_NOATTN = 0
DEBUG_NOTR = 0
DEBUG_L0F = ""
DBG_B = [None]
PARANOID = 0
W_dbg = [None, None]
W_ytmp = [None]



def fenwick_mask_np(Tn):
    m = np.zeros((Tn, Tn), dtype=bool)
    for i in range(Tn):
        j = i + 1
        while j <= Tn:
            m[i, j - 1] = True
            j += j & -j
        step = 1
        while i - step >= 0:
            m[i, i - step] = True
            step *= 2
    m &= np.tril(np.ones((Tn, Tn), dtype=bool))
    return m


def l0_local_mask_np(Tn, window=512):
    m = np.zeros((Tn, Tn), dtype=bool)
    i = np.arange(Tn)
    for q in range(Tn):
        lo = max(0, q - window)
        m[q, lo:q + 1] = True
    return m


def build_strips(mask, slot_of):
    nq = T // 128
    strips = {}
    class_of = []
    covered = np.zeros_like(mask)
    for Q in range(nq):
        slots = slot_of(Q)
        assert len(slots) == 5
        strip = np.full((128, 5 * 128), NEG, dtype=np.float32)
        for s, kt in enumerate(slots):
            if kt < 0 or kt > Q:
                continue
            sub = mask[Q * 128:(Q + 1) * 128, kt * 128:(kt + 1) * 128]
            strip[:, s * 128:(s + 1) * 128] = np.where(sub, 0.0, NEG)
            covered[Q * 128:(Q + 1) * 128, kt * 128:(kt + 1) * 128] = True
        key = strip.tobytes()
        if key not in strips:
            strips[key] = strip
        class_of.append(key)
    assert not (mask & ~covered).any(), "slot list misses allowed entries"
    keys = list(strips.keys())
    cls_idx = [keys.index(k) for k in class_of]
    arr = np.stack([strips[k] for k in keys])
    return arr, np.array(cls_idx, dtype=np.int64)


_CLS_CACHE = {}


def get_cls(kind):
    if kind not in _CLS_CACHE:
        if kind == "l0":
            _CLS_CACHE[kind] = build_strips(l0_local_mask_np(T), l0_slots)
        else:
            _CLS_CACHE[kind] = build_strips(fenwick_mask_np(T), l1_slots)
    return _CLS_CACHE[kind]


def l1_slots(Q):
    return [Q - 8, Q - 4, Q - 2, Q - 1, Q]


def l0_slots(Q):
    return [Q - 4, Q - 3, Q - 2, Q - 1, Q]


def build_rotary_tables():
    DIM_HEAD, WINDOW = 64, 512
    half_idx = np.arange(0, DIM_HEAD, 2, dtype=np.float64)
    inv_freq = 1.0 / 10000.0 ** (half_idx / DIM_HEAD)
    t = np.arange(T, dtype=np.float64)
    freqs = t[:, None] * inv_freq[None, :]
    freqs = np.concatenate([freqs, freqs], axis=-1)
    base = (half_idx + 0.4 * DIM_HEAD) / (1.4 * DIM_HEAD)
    power = (t - T // 2) / float(WINDOW // 2)
    scale = base[None, :] ** power[:, None]
    scale = np.concatenate([scale, scale], axis=-1)
    cos, sin = np.cos(freqs), np.sin(freqs)
    sgn = np.where(np.arange(64) < 32, -1.0, 1.0)[None, :]
    Aq = cos * scale
    Bq = sgn * sin * scale
    Ak = cos / scale
    Bk = sgn * sin / scale
    return np.stack([Aq, Bq, Ak, Bk]).astype(np.float32)



class Ctx:
    def __init__(self, nc, block, sems):
        self.nc = nc
        self.block = block
        self.sems = sems
        self.counts = {k: 0 for k in sems}
        self.cur = None
        self.eng = None

    def sem(self, name):
        return self.sems[name]

    def inc(self, name, n=1):
        self.counts[name] += n
        return self.counts[name]

    def val(self, name):
        return self.counts[name]

    def on(self, engine):
        return self.cur == engine

    def op(self, engine, emit_fn, inc=None, waits=()):
        if self.cur == engine:
            for s, v in waits:
                self.eng.wait_ge(self.sems[s], v)
            ins = emit_fn(self.eng)
            if inc is not None:
                ins.then_inc(self.sems[inc[0]], inc[1])
        if inc is not None:
            self.counts[inc[0]] += inc[1]
            return self.counts[inc[0]]
        return None

    def wait(self, engine, sem_name, value):
        if self.cur == engine:
            self.eng.wait_ge(self.sems[sem_name], value)

    def syncpoint(self, sems=("mm", "vec", "act", "wdma", "xdma", "gdma")):
        for s in sems:
            self.eng.wait_ge(self.sems[s], self.counts[s])

    def dma_chunks(self, queue, dst, src, nch, waits=()):
        last = None
        for c in range(nch):
            last = self.dma(queue, dst[:, c, :], src[c * 128:(c + 1) * 128, :],
                            waits=waits if c == 0 else ())
        return last

    def dma(self, queue, out, in_, sem_name=None, waits=()):
        sem_name = sem_name or {"sync": "wdma", "scalar": "xdma",
                                "gpsimd": "gdma", "vector": "vdma"}[queue]
        return self.op(queue,
                       lambda e: e.dma_start(out=out, in_=in_),
                       inc=(sem_name, 16), waits=waits)


def run_emitter(ctx, body, *args, **kwargs):
    entry = dict(ctx.counts)
    final = None
    for name in ENGINES:
        ctx.counts = dict(entry)
        ctx.cur = name
        section = getattr(ctx.block, name)

        def _run(eng, _name=name):
            ctx.eng = eng
            body(ctx, *args, **kwargs)

        section(_run)
        if final is None:
            final = dict(ctx.counts)
        else:
            assert final == ctx.counts, "emitter counts diverged between engine passes"
    ctx.counts = final
    ctx.cur = None
    ctx.eng = None


def barrier_body(ctx):
    my_q = {"sync": "wdma", "scalar": "xdma", "gpsimd": "gdma"}
    tgt = ctx.val("phase") + 5
    for name in ENGINES:
        if ctx.on(name):
            if name in my_q:
                ctx.eng.wait_ge(ctx.sems[my_q[name]], ctx.val(my_q[name]))
            ctx.eng.nop().then_inc(ctx.sems["phase"], 1)
            ctx.eng.wait_ge(ctx.sems["phase"], tgt)
    ctx.inc("phase", 5)


def barrier(ctx):
    run_emitter(ctx, barrier_body)



def emit_prep(ctx, specs, st):
    tiles = []
    for (src, dst) in specs:
        R, C = src.shape
        assert C <= st["C"], (R, C)
        nr = (R + 127) // 128
        for r in range(nr):
            p = min(128, R - r * 128)
            tiles.append((src, dst, r, p, C))
    nbuf = 2
    for i, (src, dst, r, p, C) in enumerate(tiles):
        buf = i % nbuf
        fa = st["f32"][buf]
        ba = st["bf16"][buf]
        w = []
        if i >= nbuf:
            w = [("gdma", st["gdma_after"][i - nbuf])]
        ld = ctx.dma("sync", fa[0:p, 0:C], src[r * 128:r * 128 + p, :], waits=w)
        wv = [("wdma", ld)]
        if i >= nbuf:
            wv.append(("gdma", st["gdma_after"][i - nbuf]))
        ctx.op("vector", lambda e, fa=fa, ba=ba, p=p, C=C: e.tensor_copy(
            ba[0:p, 0:C], fa[0:p, 0:C]), inc=("vec", 1), waits=wv)
        stv = ctx.dma("gpsimd", dst[r * 128:r * 128 + p, :], ba[0:p, 0:C],
                      waits=[("vec", ctx.val("vec"))])
        st["gdma_after"].append(stv)



def emit_ln(ctx, bufs, src3, b, dst_bf, dst_f32=None, affine=None):
    nc = ctx.nc
    xf = bufs["ln_xf"]
    xb = bufs["ln_xb"]
    xq = bufs["ln_xq"]
    rows = bufs["ln_rows"]
    rbf = bufs["ln_rbf"]
    ps_s = bufs["ps_a"]
    ps_q = bufs["ps_b"]
    ps_r = bufs["ps_c"]
    ps_m = bufs["ps_d"]
    ones_col = bufs["ones_colb"]
    ones_row = bufs["ones_rowb"]

    ctx.op("vector", lambda e: e.memset(dst_bf[:, :, 0:1], 0.0), inc=("vec", 1))
    if dst_f32 is not None:
        ctx.op("vector", lambda e: e.memset(dst_f32[:, :, 0:1], 0.0), inc=("vec", 1))

    src = src3[b].rearrange("(c p) t -> p c t", p=128)
    for ti in range(NTT):
        t0 = ti * TT
        for c_ in range(NCH):
            ld = ctx.dma("scalar", xf[:, c_, :], src[:, c_, t0:t0 + TT],
                         waits=([("vec", ctx.val("vec"))] if (ti > 0 and c_ == 0)
                                else ()))
        cast = ctx.op("scalar", lambda e: e.copy(xb[:, :, :], xf[:, :, :]),
                      inc=("act", 1), waits=[("xdma", ld)])
        sq = ctx.op("scalar", lambda e: e.activation(xq[:, :, :], xb[:, :, :], AF.Square),
                    inc=("act", 1))
        for c in range(NCH):
            ctx.op("tensor", lambda e, c=c: e.matmul(
                ps_s[0:1, :], ones_col[:, :], xb[:, c, :],
                start=(c == 0), stop=(c == NCH - 1)),
                inc=("mm", 1) if c == NCH - 1 else None,
                waits=[("act", cast)] if c == 0 else ())
        for c in range(NCH):
            ctx.op("tensor", lambda e, c=c: e.matmul(
                ps_q[0:1, :], ones_col[:, :], xq[:, c, :],
                start=(c == 0), stop=(c == NCH - 1)),
                inc=("mm", 1) if c == NCH - 1 else None,
                waits=[("act", sq)] if c == 0 else ())
        mmv = ctx.val("mm")
        SUM, VAR, RSTD, MR = 0, TT, 2 * TT, 3 * TT
        ctx.op("vector", lambda e: e.tensor_scalar_mul(
            rows[0:1, SUM:SUM + TT], ps_s[0:1, :], 1.0 / D),
            inc=("vec", 1), waits=[("mm", mmv)])
        ctx.op("vector", lambda e: e.tensor_mul(
            rows[0:1, VAR:VAR + TT], rows[0:1, SUM:SUM + TT], rows[0:1, SUM:SUM + TT]),
            inc=("vec", 1))
        ctx.op("vector", lambda e: e.scalar_tensor_tensor(
            rows[0:1, VAR:VAR + TT], ps_q[0:1, :], 1.0 / D, rows[0:1, VAR:VAR + TT],
            ALU.mult, ALU.subtract),
            inc=("vec", 1))
        sqv = ctx.op("scalar", lambda e: e.activation(
            rows[0:1, RSTD:RSTD + TT], rows[0:1, VAR:VAR + TT], AF.Sqrt,
            bias=bufs["misc"][0:1, 0:1]),
            inc=("act", 1), waits=[("vec", ctx.val("vec"))])
        ctx.op("vector", lambda e: e.reciprocal(
            rows[0:1, RSTD:RSTD + TT], rows[0:1, RSTD:RSTD + TT]),
            inc=("vec", 1), waits=[("act", sqv)])
        ctx.op("vector", lambda e: e.tensor_mul(
            rows[0:1, MR:MR + TT], rows[0:1, SUM:SUM + TT], rows[0:1, RSTD:RSTD + TT]),
            inc=("vec", 1))
        cbf = ctx.op("vector", lambda e: e.tensor_copy(
            rbf[0:1, :], rows[0:1, RSTD:RSTD + 2 * TT]),
            inc=("vec", 1))
        bc = ctx.op("tensor", lambda e: e.matmul(ps_r[:, :], ones_row[:, :],
                                                 rbf[0:1, 0:TT],
                                                 start=True, stop=True),
                    inc=None, waits=[("vec", cbf)])
        bc = ctx.op("tensor", lambda e: e.matmul(ps_m[:, :], ones_row[:, :],
                                                 rbf[0:1, TT:2 * TT],
                                                 start=True, stop=True),
                    inc=("mm", 1))
        for c in range(NCH):
            ctx.op("vector", lambda e, c=c: e.tensor_tensor(
                xb[:, c, :], xf[:, c, :], ps_r[:, :], ALU.mult),
                inc=("vec", 1), waits=[("mm", bc)] if c == 0 else ())
            if affine is not None:
                gcol, bcol = affine
                ctx.op("vector", lambda e, c=c: e.tensor_tensor(
                    xq[:, c, :], xb[:, c, :], ps_m[:, :], ALU.subtract),
                    inc=("vec", 1))
                ctx.op("vector", lambda e, c=c, t0=t0: e.tensor_scalar(
                    dst_bf[:, c, 1 + t0:1 + t0 + TT], xq[:, c, :],
                    bufs["smalls"][:, gcol + c:gcol + c + 1],
                    bufs["smalls"][:, bcol + c:bcol + c + 1],
                    ALU.mult, ALU.add),
                    inc=("vec", 1))
                continue
            if dst_f32 is not None:
                ctx.op("vector", lambda e, c=c, t0=t0: e.tensor_tensor(
                    dst_f32[:, c, 1 + t0:1 + t0 + TT], xb[:, c, :], ps_m[:, :], ALU.subtract),
                    inc=("vec", 1))
                ctx.op("scalar", lambda e, c=c, t0=t0: e.copy(
                    dst_bf[:, c, 1 + t0:1 + t0 + TT], dst_f32[:, c, 1 + t0:1 + t0 + TT]),
                    inc=("act", 1), waits=[("vec", ctx.val("vec"))])
            else:
                ctx.op("vector", lambda e, c=c, t0=t0: e.tensor_tensor(
                    dst_bf[:, c, 1 + t0:1 + t0 + TT], xb[:, c, :], ps_m[:, :], ALU.subtract),
                    inc=("vec", 1))
    if dst_f32 is not None:
        ctx.wait("scalar", "act", ctx.val("act"))



def emit_l0_attn(ctx, bufs, W, src3, dst3, b):
    xln = bufs["xln"]
    kv = bufs["l0a_kv"]
    qt = bufs["l0a_qt"]
    en = bufs["l0a_en"]
    ctxT = bufs["l0a_ctx"]
    ps0 = bufs["ps_a"]
    ps1 = bufs["ps_b"]
    xf = bufs["l0a_xf"]
    out_sb = bufs["l0a_out"]
    wq, wk, wv, wo = W["l0a_wq"], W["l0a_wk"], W["l0a_wv"], W["l0a_wo"]
    wsb = bufs["l0a_w"]
    wosb = bufs["l0a_wo"]

    ldw = ctx.dma_chunks("sync", wsb[:, :, 0:64].rearrange("p c d -> p c d"), wq, NCH)
    ldw = ctx.dma_chunks("sync", wsb[:, :, 64:128].rearrange("p c d -> p c d"), wk, NCH)
    ldw = ctx.dma_chunks("sync", wsb[:, :, 128:192].rearrange("p c d -> p c d"), wv, NCH)
    ldw = ctx.dma("sync", wosb[0:64, :], wo[:, :])

    for tt in range(16):
        t0 = tt * 128
        for c in range(NCH):
            w = []
            if tt == 0 and c == 0:
                w.append(("wdma", ldw))
            if c == 0:
                w.append(("vec", ctx.val("vec")))
            ctx.op("tensor", lambda e, c=c, t0=t0: e.matmul(
                ps0[:, 0:128],
                xln[:, c, 1 + t0:1 + t0 + 128],
                wsb[:, c, 64:192],
                start=(c == 0), stop=(c == NCH - 1)),
                inc=("mm", 1) if c == NCH - 1 else None, waits=w)
        mmv = ctx.val("mm")
        ctx.op("vector", lambda e, tt=tt: e.tensor_copy(kv[:, tt, :], ps0[:, 0:128]),
               inc=("vec", 1), waits=[("mm", mmv)])
    kvv = ctx.val("vec")
    for tt in range(16):
        ctx.op("tensor", lambda e, tt=tt: e.matmul(
            ps1[0:64, 0:64], kv[:, tt, 0:64], kv[:, tt, 64:128],
            start=(tt == 0), stop=(tt == 15)),
            inc=("mm", 1) if tt == 15 else None,
            waits=[("vec", kvv)] if tt == 0 else ())
    ctx.op("vector", lambda e: e.tensor_copy(en[0:64, :], ps1[0:64, 0:64]),
           inc=("vec", 1), waits=[("mm", ctx.val("mm"))])

    for ti in range(NTT):
        t0 = ti * TT
        for c in range(NCH):
            ctx.op("tensor", lambda e, c=c, t0=t0: e.matmul(
                ps0[0:64, :], wsb[:, c, 0:64], xln[:, c, 1 + t0:1 + t0 + TT],
                start=(c == 0), stop=(c == NCH - 1)),
                inc=("mm", 1) if c == NCH - 1 else None,
                waits=[("vec", ctx.val("vec"))] if c == 0 else ())
        ctx.op("vector", lambda e, t0=t0: e.tensor_copy(qt[0:64, t0:t0 + TT], ps0[0:64, :]),
               inc=("vec", 1), waits=[("mm", ctx.val("mm"))])
    qtv = ctx.val("vec")
    scale = math.sqrt(3.0 / 64.0)
    for ti in range(NTT):
        t0 = ti * TT
        w = [("vec", qtv)] if ti == 0 else [("act", ctx.val("act"))]
        ctx.op("tensor", lambda e, t0=t0: e.matmul(
            ps1[0:64, :], en[0:64, :], qt[0:64, t0:t0 + TT], start=True, stop=True),
            inc=("mm", 1), waits=w)
        ctx.op("scalar", lambda e, t0=t0: e.activation(
            ctxT[0:64, t0:t0 + TT], ps1[0:64, :], AF.Copy, scale=scale),
            inc=("act", 1), waits=[("mm", ctx.val("mm"))])
    src = src3[b].rearrange("(c p) t -> p c t", p=128)
    dst = dst3[b].rearrange("(c p) t -> p c t", p=128)
    actv = ctx.val("act")
    stores = []
    for ti in range(NTT):
        t0 = ti * TT
        for c_ in range(NCH):
            ld = ctx.dma("scalar", xf[:, c_, :], src[:, c_, t0:t0 + TT],
                         waits=([("gdma", stores[-1])] if (stores and c_ == 0)
                                else ()))
        for c in range(NCH):
            ctx.op("tensor", lambda e, c=c, t0=t0: e.matmul(
                ps0[:, :], wosb[0:64, c * 128:(c + 1) * 128], ctxT[0:64, t0:t0 + TT],
                start=True, stop=True),
                inc=("mm", 1),
                waits=[("act", actv), ("vec", ctx.val("vec"))])
            wv = [("mm", ctx.val("mm")), ("xdma", ld)]
            if c == 0 and stores:
                wv.append(("gdma", stores[-1]))
            ctx.op("vector", lambda e, c=c: e.tensor_add(
                out_sb[:, c, :], xf[:, c, :], ps0[:, :]),
                inc=("vec", 1), waits=wv)
        sv = ctx.val("vec")
        for c_ in range(NCH):
            stores.append(ctx.dma("gpsimd", dst[:, c_, t0:t0 + TT], out_sb[:, c_, :],
                                  waits=[("vec", sv)] if c_ == 0 else ()))
    ctx.wait("gpsimd", "gdma", ctx.val("gdma"))



def build_program(enable=("l0f1", "l0a", "l0f2", "moe1", "l1a", "moe2")):
    global CLS_L0, CLS_L1
    CLS_L0 = [int(v) for v in get_cls("l0")[1]]
    CLS_L1 = [int(v) for v in get_cls("l1")[1]]
    nc = bass.Bass("TRN2", target_bir_lowering=False)

    x_in = nc.dram_tensor("x_in", [B_PER_CORE, D, T], F32, kind="ExternalInput").ap()
    y_out = nc.dram_tensor("y_out", [B_PER_CORE, D, T], F32, kind="ExternalOutput").ap()
    xcur = nc.dram_tensor("xcur", [B_PER_CORE, D, T], F32, kind="Internal").ap()
    W_ytmp[0] = nc.dram_tensor("ytmp", [1, D, T], F32, kind="Internal").ap()
    dbg = nc.dram_tensor("dbg", [16, T], F32,
                         kind="ExternalOutput" if DEBUG_MOE == "router_dump"
                         else "Internal").ap()
    W_dbg[0] = dbg
    dbg2 = nc.dram_tensor("dbg2", [128, 16384], F32,
                          kind="ExternalOutput" if DEBUG_ATTN else "Internal").ap()
    W_dbg[1] = dbg2

    W = {}

    def win(name, shape):
        W[name] = nc.dram_tensor(name, shape, F32, kind="ExternalInput").ap()
        return W[name]

    win("f32_l0a_wq", [D, 64]); win("f32_l0a_wk", [D, 64])
    win("f32_l0a_wv", [D, 64]); win("f32_l0a_wo", [64, D])
    win("f32_l0f_wqkv", [2, D, 1536]); win("f32_l0f_wo", [2, 512, D])
    win("f32_l0f_w1", [2, D, 5632]); win("f32_l0f_w2", [2, 2816, D])
    win("f32_l1a_wqkv", [D, 3072]); win("f32_l1a_wproj", [D, D])
    win("f32_moe_gw", [D, 16])
    win("f32_moe_wg", [16, D, D]); win("f32_moe_wu", [16, D, D])
    win("f32_moe_wd", [16, D, D])
    win("f32_moe_sg", [D, D]); win("f32_moe_su", [D, D]); win("f32_moe_sd", [D, D])
    win("b_qkv", [128, 24])
    win("b_qkv_row", [1, 3072])
    win("b_proj", [128, 8])
    win("ln_g", [2, 128, 8]); win("ln_b", [2, 128, 8])
    win("b1", [2, 128, 44]); win("b2", [2, 128, 8])
    win("qsks_row", [1, 2048])
    win("ebias_row", [1, 16])
    win("rot", [4, T, 64])
    win("misc_row", [1, 8])
    win("ones_colf", [128, 1]); win("ones_rowf", [1, 128])
    win("identf", [128, 128])
    def winb(name, shape):
        W[name] = nc.dram_tensor(name, shape, BF16, kind="ExternalInput").ap()
        return W[name]
    winb("ones_colb", [128, 1]); winb("ones_rowb", [1, 128])
    winb("identb", [128, 128])
    winb("l0_strips", [5, 128, 640]); winb("l1_strips", [5, 128, 640])

    def wcache(name, shape):
        W[name] = nc.dram_tensor(name, shape, BF16, kind="Internal").ap()
        return W[name]
    wcache("l0a_wq", [D, 64]); wcache("l0a_wk", [D, 64])
    wcache("l0a_wv", [D, 64]); wcache("l0a_wo", [64, D])
    wcache("l0f_wqkv", [2, D, 1536]); wcache("l0f_wo", [2, 512, D])
    wcache("l0f_w1", [2, D, 5632]); wcache("l0f_w2", [2, 2816, D])
    wcache("l1a_wqkv", [D, 3072]); wcache("l1a_wproj", [D, D])
    wcache("moe_gw", [D, 16])
    wcache("moe_wg", [16, D, D]); wcache("moe_wu", [16, D, D])
    wcache("moe_wd", [16, D, D])
    wcache("moe_sg", [D, D]); wcache("moe_su", [D, D]); wcache("moe_sd", [D, D])

    prep_specs = []
    for nm in ["l0a_wq", "l0a_wk", "l0a_wv", "l0a_wo", "l1a_wqkv", "l1a_wproj",
               "moe_gw", "moe_sg", "moe_su", "moe_sd"]:
        prep_specs.append((W["f32_" + nm], W[nm]))
    for l in range(2):
        for nm in ["l0f_wqkv", "l0f_wo", "l0f_w1", "l0f_w2"]:
            prep_specs.append((W["f32_" + nm][l], W[nm][l]))
    for e in range(16):
        for nm in ["moe_wg", "moe_wu", "moe_wd"]:
            prep_specs.append((W["f32_" + nm][e], W[nm][e]))

    with (
        nc.semaphore("wdma") as s_wdma,
        nc.semaphore("xdma") as s_xdma,
        nc.semaphore("gdma") as s_gdma,
        nc.semaphore("vdma") as s_vdma,
        nc.semaphore("mm") as s_mm,
        nc.semaphore("vec") as s_vec,
        nc.semaphore("act") as s_act,
        nc.semaphore("gp") as s_gp,
        nc.semaphore("phase") as s_phase,
        nc.Block() as block,
    ):
        sems = {"wdma": s_wdma, "xdma": s_xdma, "gdma": s_gdma, "vdma": s_vdma,
                "mm": s_mm, "vec": s_vec, "act": s_act, "gp": s_gp,
                "phase": s_phase}
        ctx = Ctx(nc, block, sems)

        with (
            nc.sbuf_tensor("prep_f0", [128, 5632], F32) as pf0,
            nc.sbuf_tensor("prep_f1", [128, 5632], F32) as pf1,
            nc.sbuf_tensor("prep_b0", [128, 5632], BF16) as pb0,
            nc.sbuf_tensor("prep_b1", [128, 5632], BF16) as pb1,
        ):
            st = {"f32": [pf0, pf1], "bf16": [pb0, pb1], "C": 5632,
                  "gdma_after": []}
            run_emitter(ctx, emit_prep, prep_specs, st)
            barrier(ctx)

        with (
            SB(nc, "sb_ones_colb", [128, 1], BF16) as ones_colb,
            SB(nc, "sb_ones_rowb", [1, 128], BF16) as ones_rowb,
            SB(nc, "sb_ones_rowf", [1, 128], F32) as ones_rowf,
            SB(nc, "sb_identb", [128, 128], BF16) as identb,
            SB(nc, "sb_identf", [128, 128], F32) as identf,
            nc.sbuf_tensor("smalls", [128, 256], F32) as smalls,
            nc.sbuf_tensor("qsks", [1, 2048], F32) as qsks,
            nc.sbuf_tensor("ebias", [1, 16], F32) as ebias,
            SB(nc, "sb_misc", [1, 8], F32) as miscrow,
        ):
            def load_consts(c):
                c.dma("sync", ones_colb[:, :], W["ones_colb"][:, :])
                c.dma("sync", ones_rowb[:, :], W["ones_rowb"][:, :])
                c.dma("sync", ones_rowf[:, :], W["ones_rowf"][:, :])
                c.dma("sync", identb[:, :], W["identb"][:, :])
                c.dma("sync", identf[:, :], W["identf"][:, :])
                c.dma("sync", smalls[:, 0:24], W["b_qkv"][:, :])
                c.dma("sync", smalls[:, 24:32], W["b_proj"][:, :])
                for l in range(2):
                    c.dma("sync", smalls[:, 32 + 8 * l:40 + 8 * l], W["ln_g"][l])
                    c.dma("sync", smalls[:, 48 + 8 * l:56 + 8 * l], W["ln_b"][l])
                    c.dma("sync", smalls[:, 64 + 44 * l:108 + 44 * l], W["b1"][l])
                    c.dma("sync", smalls[:, 152 + 8 * l:160 + 8 * l], W["b2"][l])

                c.dma("sync", qsks[:, :], W["qsks_row"][:, :])
                c.dma("sync", ebias[:, :], W["ebias_row"][:, :])
                c.dma("sync", miscrow[:, :], W["misc_row"][:, :])
                c.wait("sync", "wdma", c.val("wdma"))
            run_emitter(ctx, lambda c: load_consts(c))
            barrier(ctx)

            consts = {"ones_colb": ones_colb, "ones_rowb": ones_rowb,
                      "ones_rowf": ones_rowf, "identb": identb, "identf": identf,
                      "smalls": smalls, "qsks": qsks, "ebias": ebias,
                      "misc": miscrow}

            for b in range(B_PER_CORE):
                seq_sublayers(ctx, nc, W, consts, x_in, xcur, y_out, b, enable)

            barrier(ctx)

    return nc


def seq_sublayers(ctx, nc, W, consts, x_in, xcur, y_out, b, enable):
    steps = []
    cur = x_in
    for i, name in enumerate(("l0f1", "l0a", "l0f2", "moe1", "l1a", "moe2")):
        dst = y_out if name == "moe2" else xcur
        steps.append((name, cur, dst))
        cur = dst

    for name, src3, dst3 in steps:
        if name not in enable:
            if src3 is not dst3:
                emit_copy_seq(ctx, src3, dst3, b)
            continue
        if name == "l0a":
            with sublayer_bufs_l0a(nc) as bufs:
                bufs.update(consts)
                with ln_scratch(nc) as lnb:
                    run_emitter(ctx, emit_ln, {**bufs, **lnb}, src3, b, bufs["xln"])
                    barrier(ctx)
                with l0a_scratch(nc) as ab:
                    run_emitter(ctx, emit_l0_attn, {**bufs, **ab}, W, src3, dst3, b)
                    barrier(ctx)
        elif name in ("moe1", "moe2"):
            with sublayer_bufs_moe(nc) as bufs:
                bufs.update(consts)
                with ln_scratch(nc) as lnb:
                    run_emitter(ctx, emit_ln, {**bufs, **lnb}, src3, b, bufs["xln"])
                    barrier(ctx)
                with moe_router_scratch(nc) as rb:
                    run_emitter(ctx, emit_moe_router, {**bufs, **rb}, W)
                    barrier(ctx)
                if DEBUG_MOE == "router_dump" and b == 0:
                    def _dump(c):
                        c.dma("gpsimd", W_dbg[0][:, :], bufs["combT"][:, :])
                        c.wait("gpsimd", "gdma", c.val("gdma"))
                    run_emitter(ctx, _dump)
                    barrier(ctx)
                with moe_expert_scratch(nc) as eb:
                    run_emitter(ctx, emit_moe_experts, {**bufs, **eb}, W)
                    barrier(ctx)
                with epilogue_scratch(nc) as pb:
                    run_emitter(ctx, emit_epilogue, {**bufs, **pb}, src3, dst3, b,
                                bufs["yacc"], 0.5)
                    barrier(ctx)
        elif name == "l1a":
            with sublayer_bufs_l1a(nc) as bufs:
                bufs.update(consts)
                with ln_scratch(nc) as lnb:
                    run_emitter(ctx, emit_ln, {**bufs, **lnb}, src3, b, bufs["xln"])
                    barrier(ctx)
                cfg = l1_cfg()
                cfg["cls_of"] = CLS_L1
                with attn_scratch(nc, cfg) as ab:
                    run_emitter(ctx, emit_banded_attn, {**bufs, **ab}, W, cfg,
                                src3, dst3, b)
                    barrier(ctx)
        else:
            run_l0f_sublayer(ctx, nc, W, consts, src3, dst3, b)


def run_l0f_sublayer(ctx, nc, W, consts, src3, dst3, b):
    DBG_B[0] = b
    with sublayer_bufs_l0f(nc) as bufs:
        bufs.update(consts)
        hoisted = contextlib.ExitStack()
        if "hoist" in DEBUG_L0F:
            sb2 = hoisted.enter_context(l0f_stats_scratch(nc))
            h2nh = [hoisted.enter_context(SB(nc, "sb_h2nh%d" % c, [128, T], BF16))
                    for c in range(NCH)]
            bufs["_hoist"] = (sb2, h2nh)
        with ln_scratch(nc) as lnb:
            run_emitter(ctx, emit_ln, {**bufs, **lnb}, src3, b, bufs["ybf"], bufs["y"])
            barrier(ctx)
        nlayers = 1 if "onelayer" in DEBUG_L0F else 2
        for l in range(nlayers):
            if "noattn" not in DEBUG_L0F:
                cfg = l0_cfg(l)
                cfg["cls_of"] = CLS_L0
                with attn_scratch(nc, cfg) as ab:
                    run_emitter(ctx, emit_banded_attn, {**bufs, **ab}, W, cfg,
                                src3, dst3, b)
                    barrier(ctx)
            if "noffn" not in DEBUG_L0F:
                run_l0f_ffn_half(ctx, nc, W, bufs, l)
            if l == 0:
                run_emitter(ctx, emit_ybf_refresh, bufs)
                barrier(ctx)
        with epilogue_scratch(nc) as pb:
            run_emitter(ctx, emit_epilogue, {**bufs, **pb}, src3, dst3, b,
                        bufs["y"][:, :, 1:1 + T], 0.5)
            barrier(ctx)


def run_l0f_ffn_half(ctx, nc, W, bufs, l):
    y = bufs["y"]

    def spill(c):
        ld = None
        for c_ in range(NCH):
            off = 1 if c_ < 4 else 0
            for hh in range(2):
                ld = c.dma("gpsimd",
                           W_ytmp[0][0, c_ * 128:(c_ + 1) * 128,
                                     hh * 1024:(hh + 1) * 1024],
                           y[:, c_, off + hh * 1024:off + hh * 1024 + 1024])
        c.wait("gpsimd", "gdma", c.val("gdma"))
    run_emitter(ctx, spill)
    barrier(ctx)
    with SB(nc, "sb_hg", [128, NCH, GW], BF16) as hg:
        hb = dict(bufs)
        hb["h2n"] = hg
        with ln_scratch(nc) as lnb:
            run_emitter(ctx, emit_ln, {**hb, **lnb}, W_ytmp[0], 0, hg,
                        affine=(32 + 8 * l, 48 + 8 * l))
            barrier(ctx)
        with l0f_ffn_scratch(nc) as fb:
            run_emitter(ctx, emit_l0f_ffn, {**hb, **fb}, W, l)
            barrier(ctx)


def emit_copy_seq(ctx, src3, dst3, b):
    def body(c):
        c.dma("gpsimd", dst3[b], src3[b])
        c.wait("gpsimd", "gdma", c.val("gdma"))
    run_emitter(ctx, body)
    barrier(ctx)



import contextlib


def SB(nc, name, shape, dtype):
    return nc.sbuf_tensor(name, shape, dtype, allow_name_mangling=True)


def PS(nc, name, shape, dtype=F32):
    return nc.psum_tensor(name, shape, dtype, allow_name_mangling=True)


@contextlib.contextmanager
def sublayer_bufs_l0a(nc):
    with SB(nc, "sb_xln", [128, NCH, GW], BF16) as xln:
        yield {"xln": xln}


@contextlib.contextmanager
def l0a_scratch(nc):
    with (
        SB(nc, "sb_l0a_kv", [128, 16, 128], BF16) as kvb,
        SB(nc, "sb_l0a_qt", [64, T], BF16) as qt,
        SB(nc, "sb_l0a_en", [64, 64], BF16) as en,
        SB(nc, "sb_l0a_ctx", [64, T], BF16) as ctxT,
        SB(nc, "sb_l0a_out", [128, NCH, TT], F32) as out_sb,
        SB(nc, "sb_l0a_xf", [128, NCH, TT], F32) as xf,
        SB(nc, "sb_l0a_w", [128, NCH, 192], BF16) as wsb,
        SB(nc, "sb_l0a_wo", [64, D], BF16) as wosb,
        PS(nc, "ps__ps_a", [128, TT], F32) as ps_a,
        PS(nc, "ps__ps_b", [128, TT], F32) as ps_b,
    ):
        yield {"l0a_kv": kvb, "l0a_qt": qt, "l0a_en": en, "l0a_ctx": ctxT,
               "l0a_out": out_sb, "l0a_xf": xf, "l0a_w": wsb, "l0a_wo": wosb,
               "ps_a": ps_a, "ps_b": ps_b}


@contextlib.contextmanager
def ln_scratch(nc):
    with (
        SB(nc, "sb_ln_xf", [128, NCH, TT], F32) as ln_xf,
        SB(nc, "sb_ln_xb", [128, NCH, TT], BF16) as ln_xb,
        SB(nc, "sb_ln_xq", [128, NCH, TT], BF16) as ln_xq,
        SB(nc, "sb_ln_rows", [1, 4 * TT], F32) as ln_rows,
        SB(nc, "sb_ln_rbf", [1, 2 * TT], BF16) as ln_rbf,
        PS(nc, "ps__a", [128, TT], F32) as ps_a,
        PS(nc, "ps__b", [128, TT], F32) as ps_b,
        PS(nc, "ps__c", [128, TT], F32) as ps_c,
        PS(nc, "ps__d", [128, TT], F32) as ps_d,
    ):
        yield {"ln_xf": ln_xf, "ln_xb": ln_xb, "ln_xq": ln_xq,
               "ln_rows": ln_rows, "ln_rbf": ln_rbf,
               "ps_a": ps_a, "ps_b": ps_b, "ps_c": ps_c, "ps_d": ps_d}


@contextlib.contextmanager
def epilogue_scratch(nc):
    with (
        SB(nc, "sb_ep_xf", [128, NCH, TT], F32) as xf,
        SB(nc, "sb_ep_out", [128, NCH, TT], F32) as out,
    ):
        yield {"ep_xf": xf, "ep_out": out}


def emit_epilogue(ctx, bufs, src3, dst3, b, contrib, alpha):
    xf = bufs["ep_xf"]
    out = bufs["ep_out"]
    src = src3[b].rearrange("(c p) t -> p c t", p=128)
    dst = dst3[b].rearrange("(c p) t -> p c t", p=128)
    stores = []
    for ti in range(NTT):
        t0 = ti * TT
        for c_ in range(NCH):
            ld = ctx.dma("scalar", xf[:, c_, :], src[:, c_, t0:t0 + TT],
                         waits=([("gdma", stores[-1])] if (stores and c_ == 0)
                                else ()))
        for c in range(NCH):
            wv = [("xdma", ld)]
            if c == 0 and stores:
                wv.append(("gdma", stores[-1]))
            ctx.op("vector", lambda e, c=c, t0=t0: e.scalar_tensor_tensor(
                out[:, c, :], contrib[:, c, t0:t0 + TT], alpha, xf[:, c, :],
                ALU.mult, ALU.add),
                inc=("vec", 1), waits=wv)
        sv = ctx.val("vec")
        for c_ in range(NCH):
            stores.append(ctx.dma("gpsimd", dst[:, c_, t0:t0 + TT], out[:, c_, :],
                                  waits=[("vec", sv)] if c_ == 0 else ()))
    ctx.wait("gpsimd", "gdma", ctx.val("gdma"))


@contextlib.contextmanager
def sublayer_bufs_moe(nc):
    with (
        SB(nc, "sb_xln", [128, NCH, GW], BF16) as xln,
        SB(nc, "sb_yacc", [128, NCH, T], F32) as yacc,
        SB(nc, "sb_combT", [16, T], F32) as combT,
    ):
        yield {"xln": xln, "yacc": yacc, "combT": combT}


@contextlib.contextmanager
def moe_router_scratch(nc):
    with (
        SB(nc, "sb_gw", [128, NCH, 16], BF16) as gw,
        SB(nc, "sb_r16a", [128, 16], F32) as r16a,
        SB(nc, "sb_r16b", [128, 16], F32) as r16b,
        SB(nc, "sb_r16c", [128, 16], F32) as r16c,
        SB(nc, "sb_r16d", [128, 16], F32) as r16d,
        SB(nc, "sb_r16e", [128, 16], F32) as r16e,
        SB(nc, "sb_ebb", [128, 16], F32) as ebb,
        SB(nc, "sb_rdn", [128, 1], F32) as rdn,
        PS(nc, "ps__rg", [128, 512], F32) as ps_rg,
        PS(nc, "ps__rt", [128, 512], F32) as ps_rt,
    ):
        yield {"gw": gw, "sc": r16a, "selb": r16b, "tmp16": r16c, "cnt": r16d,
               "comb": r16e, "ebb": ebb, "rdn": rdn,
               "ps_rg": ps_rg, "ps_rt": ps_rt}


@contextlib.contextmanager
def moe_expert_scratch(nc):
    with (
        SB(nc, "sb_wg", [128, NCH, D], BF16) as wg,
        SB(nc, "sb_wu", [128, NCH, D], BF16) as wu,
        SB(nc, "sb_wd", [128, NCH, D], BF16) as wd,
        SB(nc, "sb_hbf", [128, NCH, TT], BF16) as hbf,
        SB(nc, "sb_sg", [128, TT], F32) as sg,
        SB(nc, "sb_crow", [1, T], F32) as crow,
        PS(nc, "ps__cb", [128, TT], F32) as ps_cb,
        PS(nc, "ps__g", [128, TT], F32) as ps_g,
        PS(nc, "ps__u", [128, TT], F32) as ps_u,
        PS(nc, "ps__y0", [128, TT], F32) as ps_y0,
        PS(nc, "ps__y1", [128, TT], F32) as ps_y1,
    ):
        yield {"wg": wg, "wu": wu, "wd": wd, "hbf": hbf, "sgb": sg, "crow": crow,
               "ps_cb": ps_cb, "ps_g": ps_g, "ps_u": ps_u,
               "ps_y": [ps_y0, ps_y1]}


def emit_moe_router(ctx, bufs, W):
    xln, combT = bufs["xln"], bufs["combT"]
    gw, sc, selb, tmp16 = bufs["gw"], bufs["sc"], bufs["selb"], bufs["tmp16"]
    cnt, comb, ebb, rdn = bufs["cnt"], bufs["comb"], bufs["ebb"], bufs["rdn"]
    ps_rg, ps_rt = bufs["ps_rg"], bufs["ps_rt"]
    onesf = bufs["ones_rowf"]
    identf = bufs["identf"]
    ebias = bufs["ebias"]

    ldg = ctx.dma_chunks("sync", gw, W["moe_gw"], NCH)
    ctx.op("tensor", lambda e: e.matmul(ps_rt[:, 0:16], onesf[:, :], ebias[:, :],
                                        start=True, stop=True),
           inc=("mm", 1))
    ctx.op("vector", lambda e: e.tensor_copy(ebb[:, :], ps_rt[:, 0:16]),
           inc=("vec", 1), waits=[("mm", ctx.val("mm"))])
    for tt in range(16):
        t0 = tt * 128
        for c in range(NCH):
            w = []
            if c == 0:
                w.append(("act", ctx.val("act")))
                if tt == 0:
                    w.append(("wdma", ldg))
            ctx.op("tensor", lambda e, c=c, t0=t0: e.matmul(
                ps_rg[:, 0:16], xln[:, c, 1 + t0:1 + t0 + 128], gw[:, c, :],
                start=(c == 0), stop=(c == NCH - 1)),
                inc=("mm", 1) if c == NCH - 1 else None, waits=w)
        ctx.op("scalar", lambda e: e.activation(sc[:, :], ps_rg[:, 0:16], AF.Sigmoid),
               inc=("act", 1), waits=[("mm", ctx.val("mm")), ("vec", ctx.val("vec"))])
        ctx.op("vector", lambda e: e.tensor_add(selb[:, :], sc[:, :], ebb[:, :]),
               inc=("vec", 1),
               waits=[("act", ctx.val("act")), ("vec", ctx.val("vec"))])
        for e_ in range(16):
            ctx.op("vector", lambda e, e_=e_: e.tensor_single_scalar(
                tmp16[:, :], selb[:, :], selb[:, e_:e_ + 1], ALU.is_gt),
                inc=("vec", 1), waits=[("vec", ctx.val("vec"))])
            ctx.op("vector", lambda e, e_=e_: e.reduce_sum(
                cnt[:, e_:e_ + 1], tmp16[:, :], axis=AX.X),
                inc=("vec", 1), waits=[("vec", ctx.val("vec"))])
        ctx.op("vector", lambda e: e.tensor_single_scalar(
            tmp16[:, :], cnt[:, :], 7.5, ALU.is_lt), inc=("vec", 1),
            waits=[("vec", ctx.val("vec"))])
        ctx.op("vector", lambda e: e.tensor_mul(comb[:, :], sc[:, :], tmp16[:, :]),
               inc=("vec", 1), waits=[("vec", ctx.val("vec"))])
        ctx.op("vector", lambda e: e.reduce_sum(rdn[:, :], comb[:, :], axis=AX.X),
               inc=("vec", 1), waits=[("vec", ctx.val("vec"))])
        ctx.op("vector", lambda e: e.reciprocal(rdn[:, :], rdn[:, :]),
               inc=("vec", 1), waits=[("vec", ctx.val("vec"))])
        cv = ctx.op("vector", lambda e: e.tensor_scalar(
            comb[:, :], comb[:, :], rdn[:, :], 2.826, ALU.mult, ALU.mult),
            inc=("vec", 1), waits=[("vec", ctx.val("vec"))])
        ctx.op("tensor", lambda e: e.transpose(ps_rt[0:16, 0:128], comb[:, :],
                                               identf[:, :]),
               inc=("mm", 1), waits=[("vec", cv)])
        ctx.op("vector", lambda e, t0=t0: e.tensor_copy(
            combT[0:16, t0:t0 + 128], ps_rt[0:16, 0:128]),
            inc=("vec", 1), waits=[("mm", ctx.val("mm"))])



def emit_moe_experts(ctx, bufs, W):
    xln, yacc, crow, combT = bufs["xln"], bufs["yacc"], bufs["crow"], bufs["combT"]
    wg, wu, wd, hbf, sgb = bufs["wg"], bufs["wu"], bufs["wd"], bufs["hbf"], bufs["sgb"]
    ps_cb, ps_g, ps_u, ps_y = bufs["ps_cb"], bufs["ps_g"], bufs["ps_u"], bufs["ps_y"]
    onesf = bufs["ones_rowf"]

    zv = ctx.op("vector", lambda e: e.memset(yacc[:, :, :], 0.0), inc=("vec", 1))

    def one_expert(eidx, wsrc_g, wsrc_u, wsrc_d, routed):
        wmm = [("mm", ctx.val("mm"))]
        if routed:
            lr = ctx.dma("gpsimd", crow[0:1, :], combT[eidx:eidx + 1, :], waits=wmm)
        l1 = ctx.dma_chunks("sync", wg, wsrc_g, NCH, waits=wmm)
        l2 = ctx.dma_chunks("sync", wu, wsrc_u, NCH)
        l3 = ctx.dma_chunks("sync", wd, wsrc_d, NCH)
        for st in range(NTT):
            t0 = st * TT
            if routed:
                ctx.op("tensor", lambda e, t0=t0: e.matmul(
                    ps_cb[:, :], onesf[:, :],
                    crow[0:1, t0:t0 + TT],
                    start=True, stop=True),
                    inc=("mm", 1),
                    waits=[("vec", ctx.val("vec")), ("gdma", lr)])
            for ic in range(NCH):
                wgw = [("act", ctx.val("act"))]
                if ic == 0 and st == 0:
                    wgw.append(("wdma", l3))
                ctx.op("tensor", lambda e, ic=ic, t0=t0: e.matmul(
                    ps_g[:, :], wg[:, 0, ic * 128:(ic + 1) * 128], xln[:, 0, 1 + t0:1 + t0 + TT],
                    start=True, stop=False), inc=None, waits=wgw)
                for dk in range(1, NCH):
                    ctx.op("tensor", lambda e, ic=ic, dk=dk, t0=t0: e.matmul(
                        ps_g[:, :], wg[:, dk, ic * 128:(ic + 1) * 128],
                        xln[:, dk, 1 + t0:1 + t0 + TT],
                        start=False, stop=(dk == NCH - 1)),
                        inc=("mm", 1) if dk == NCH - 1 else None)
                guard_u = [("vec", ctx.val("vec"))]
                for dk in range(NCH):
                    ctx.op("tensor", lambda e, ic=ic, dk=dk, t0=t0: e.matmul(
                        ps_u[:, :], wu[:, dk, ic * 128:(ic + 1) * 128],
                        xln[:, dk, 1 + t0:1 + t0 + TT],
                        start=(dk == 0), stop=(dk == NCH - 1)),
                        inc=("mm", 1) if dk == NCH - 1 else None,
                        waits=guard_u if dk == 0 else ())
                mm_gu = ctx.val("mm")
                sv = ctx.op("scalar", lambda e: e.activation(sgb[:, :], ps_g[:, :], AF.Silu),
                            inc=("act", 1),
                            waits=[("mm", mm_gu), ("vec", ctx.val("vec"))])
                if routed:
                    ctx.op("vector", lambda e: e.tensor_mul(sgb[:, :], sgb[:, :], ps_u[:, :]),
                           inc=("vec", 1), waits=[("mm", mm_gu), ("act", sv)])
                    ctx.op("vector", lambda e, ic=ic: e.tensor_tensor(
                        hbf[:, ic, :], sgb[:, :], ps_cb[:, :], ALU.mult),
                        inc=("vec", 1))
                else:
                    ctx.op("vector", lambda e, ic=ic: e.tensor_tensor(
                        hbf[:, ic, :], sgb[:, :], ps_u[:, :], ALU.mult),
                        inc=("vec", 1), waits=[("mm", mm_gu), ("act", sv)])
            hv = ctx.val("vec")
            for dc in range(NCH):
                pb = ps_y[dc % 2]
                w = [("vec", hv if dc < 2 else ctx.val("vec"))]
                for ic in range(NCH):
                    ctx.op("tensor", lambda e, dc=dc, ic=ic, pb=pb: e.matmul(
                        pb[:, :], wd[:, ic, dc * 128:(dc + 1) * 128], hbf[:, ic, :],
                        start=(ic == 0), stop=(ic == NCH - 1)),
                        inc=("mm", 1) if ic == NCH - 1 else None,
                        waits=w if ic == 0 else ())
                ctx.op("vector", lambda e, dc=dc, t0=t0, pb=pb: e.tensor_add(
                    yacc[:, dc, t0:t0 + TT], yacc[:, dc, t0:t0 + TT], pb[:, :]),
                    inc=("vec", 1), waits=[("mm", ctx.val("mm"))])

    ctx.wait("tensor", "vec", zv)
    routed_flag = DEBUG_MOE != "unweighted"
    if DEBUG_MOE != "shared_only":
        for e_ in range(16):
            one_expert(e_, W["moe_wg"][e_], W["moe_wu"][e_], W["moe_wd"][e_],
                       routed_flag)
    one_expert(0, W["moe_sg"], W["moe_su"], W["moe_sd"], False)



@contextlib.contextmanager
def sublayer_bufs_l1a(nc):
    with SB(nc, "sb_xln", [128, NCH, GW], BF16) as xln:
        yield {"xln": xln}


@contextlib.contextmanager
def sublayer_bufs_l0f(nc):
    with (
        SB(nc, "sb_y", [128, NCH, GW], F32) as y,
        SB(nc, "sb_ybf", [128, NCH, GW], BF16) as ybf,
    ):
        yield {"y": y, "ybf": ybf}


@contextlib.contextmanager
def attn_scratch(nc, cfg):
    ncp, ring, qkvw = cfg["ncp"], cfg["ring"], cfg["qkvw"]
    nh = 2 * ncp
    with contextlib.ExitStack() as st:
        def A(name, shape, dtype):
            return st.enter_context(SB(nc, name, shape, dtype))

        def P_(name, shape, dtype=F32):
            return st.enter_context(PS(nc, name, shape, dtype))

        d = {
            "wqkv": A("sb_wqkv", [128, NCH, qkvw], BF16),
            "wpr": A("sb_wpr", [128, ncp, D], BF16),
            "strips": A("sb_strips", [128, 5 * 640], BF16),
            "kring": A("sb_kring", [128, ring, ncp, 128], BF16),
            "vring": A("sb_vring", [128, ring, nh * 64], BF16),
            "qtt": A("sb_qtt", [128, ncp, 128], BF16),
            "OT": A("sb_OT", [128, ncp, TT], BF16),
            "tok": A("sb_tok", [128, nh * 64], BF16),
            "Ssb": A("sb_Ssb", [128, 640], F32),
            "Asb": A("sb_Asb", [128, 640], BF16),
            "ATsb": A("sb_ATsb", [128, 640], BF16),
            "arow": A("sb_arow", [128, 4], F32),
            "ps_qkv": P_("ps__qkv", [128, TT]),
            "ps_sA": P_("ps__sA", [128, 640]),
            "ps_sB": P_("ps__sB", [128, 640]),
            "ps_tr": P_("ps__tr", [128, 640], BF16),
            "ps_o": P_("ps__o", [128, 128]),
            "ps_pr": P_("ps__pr", [128, TT]),
        }
        if cfg["kind"] == "l1":
            d["bqr"] = A("sb_bqr", [1, 3072], F32)
            d["bb"] = A("sb_bb", [128, 3072], BF16)
            d["prxf"] = A("sb_prxf", [128, NCH, TT], F32)
        else:
            d["rott"] = A("sb_rott", [128, 4, 64], F32)
            d["qsb"] = A("sb_qsb", [128, TT], F32)
            d["ksb"] = A("sb_ksb", [128, TT], F32)
            d["t1"] = A("sb_t1", [128, TT], F32)
            d["t2"] = A("sb_t2", [128, TT], F32)
            d["t3"] = A("sb_t3", [128, TT], F32)
            d["sq"] = A("sb_sq", [128, TT], F32)
            d["ss8"] = A("sb_ss8", [128, 8], F32)
            d["inv8"] = A("sb_inv8", [128, 8], F32)
        yield d


def l0_cfg(l):
    return {"kind": "l0", "ncp": 4, "ring": 6, "qkvw": 1536, "escale": 8.0,
            "shift": True, "l": l, "slots": l0_slots}


def l1_cfg():
    return {"kind": "l1", "ncp": 8, "ring": 10, "qkvw": 3072, "escale": 0.125,
            "shift": False, "l": None, "slots": l1_slots}


def emit_banded_attn(ctx, bufs, W, cfg, src3, dst3, b):
    kind, ncp, ring = cfg["kind"], cfg["ncp"], cfg["ring"]
    nh, qkvw, escale = 2 * ncp, cfg["qkvw"], cfg["escale"]
    nblk = qkvw // TT
    src = bufs["xln"] if kind == "l1" else bufs["ybf"]
    wqkv, wpr, strips = bufs["wqkv"], bufs["wpr"], bufs["strips"]
    kring, vring, qtt, OT, tok = (bufs["kring"], bufs["vring"], bufs["qtt"],
                                  bufs["OT"], bufs["tok"])
    Ssb, Asb, ATsb, arow = bufs["Ssb"], bufs["Asb"], bufs["ATsb"], bufs["arow"]
    ps_qkv, ps_sA, ps_sB = bufs["ps_qkv"], bufs["ps_sA"], bufs["ps_sB"]
    ps_tr, ps_o, ps_pr = bufs["ps_tr"], bufs["ps_o"], bufs["ps_pr"]
    identb = bufs["identb"]
    onesf = bufs["ones_rowf"]
    smalls = bufs["smalls"]
    cls_of = cfg["cls_of"]

    if kind == "l1":
        wq_src = W["l1a_wqkv"]
        wp_src = W["l1a_wproj"]
        strip_src = W["l1_strips"]
    else:
        wq_src = W["l0f_wqkv"][cfg["l"]]
        wp_src = W["l0f_wo"][cfg["l"]]
        strip_src = W["l0_strips"]
    ldw = ctx.dma_chunks("sync", wqkv, wq_src, NCH)
    ldw = ctx.dma_chunks("sync", wpr, wp_src, ncp)
    for s_ in range(5):
        ldw = ctx.dma("sync", strips[:, :].rearrange("p (s n) -> p s n", s=5)[:, s_, :],
                      strip_src[s_])
    mz = ctx.op("vector", lambda e: e.memset(kring[:, :, :, :], 0.0), inc=("vec", 1))
    mz = ctx.op("vector", lambda e: e.memset(vring[:, :, :], 0.0), inc=("vec", 1))
    if kind == "l1":
        bqr, bb = bufs["bqr"], bufs["bb"]
        ldb = ctx.dma("sync", bqr[:, :], W["b_qkv_row"][:, :])
        for i in range(6):
            ctx.op("tensor", lambda e, i=i: e.matmul(
                ps_pr[:, :], onesf[:, :], bqr[0:1, i * TT:(i + 1) * TT],
                start=True, stop=True),
                inc=("mm", 1),
                waits=[("wdma", ldb), ("vec", ctx.val("vec"))] if i == 0
                else [("vec", ctx.val("vec"))])
            ctx.op("vector", lambda e, i=i: e.tensor_copy(
                bb[:, i * TT:(i + 1) * TT], ps_pr[:, :]),
                inc=("vec", 1), waits=[("mm", ctx.val("mm"))])
    else:
        qsb, ksb = bufs["qsb"], bufs["ksb"]
        qsks = bufs["qsks"]
        off = 1024 * cfg["l"]
        for dst_sb, o in ((qsb, off), (ksb, off + 512)):
            ctx.op("tensor", lambda e, dst_sb=dst_sb, o=o: e.matmul(
                ps_pr[:, :], onesf[:, :], qsks[0:1, o:o + TT],
                start=True, stop=True),
                inc=("mm", 1), waits=[("vec", ctx.val("vec"))])
            ctx.op("vector", lambda e, dst_sb=dst_sb: e.tensor_copy(
                dst_sb[:, :], ps_pr[:, :]),
                inc=("vec", 1), waits=[("mm", ctx.val("mm"))])

    def l0_prep(role, tt):
        t1, t2, t3, sq = bufs["t1"], bufs["t2"], bufs["t3"], bufs["sq"]
        ss8, inv8, rott = bufs["ss8"], bufs["inv8"], bufs["rott"]
        scl = bufs["qsb"] if role == "q" else bufs["ksb"]
        ridx = 0 if role == "q" else 2
        mmv = ctx.val("mm")
        sqv = ctx.op("scalar", lambda e: e.activation(sq[:, :], ps_qkv[:, :], AF.Square),
                     inc=("act", 1),
                     waits=[("mm", mmv), ("vec", ctx.val("vec"))])
        ctx.op("vector", lambda e: e.reduce_sum(
            ss8[:, :], sq[:, :].rearrange("p (h d) -> p h d", h=8), axis=AX.X),
            inc=("vec", 1), waits=[("act", sqv)])
        sr = ctx.op("scalar", lambda e: e.activation(inv8[:, :], ss8[:, :], AF.Sqrt),
                    inc=("act", 1), waits=[("vec", ctx.val("vec"))])
        ctx.op("vector", lambda e: e.tensor_scalar_max(inv8[:, :], inv8[:, :], 1e-12),
               inc=("vec", 1), waits=[("act", sr)])
        ctx.op("vector", lambda e: e.reciprocal(inv8[:, :], inv8[:, :]),
               inc=("vec", 1), waits=[("vec", ctx.val("vec"))])
        rv = ctx.val("vec")
        ctx.op("vector", lambda e: e.tensor_tensor(t1[:, :], ps_qkv[:, :], scl[:, :],
                                                   ALU.mult),
               inc=("vec", 1), waits=[("vec", rv), ("mm", mmv)])
        ctx.op("vector", lambda e: e.tensor_tensor(
            t1[:, :].rearrange("p (h d) -> p h d", h=8),
            t1[:, :].rearrange("p (h d) -> p h d", h=8),
            inv8[:, :].unsqueeze(2).broadcast_to([128, 8, 64]), ALU.mult),
            inc=("vec", 1), waits=[("vec", ctx.val("vec"))])
        A_b = rott[:, ridx:ridx + 1, :].broadcast_to([128, 8, 64])
        B_b = rott[:, ridx + 1:ridx + 2, :].broadcast_to([128, 8, 64])
        t1v = t1[:, :].rearrange("p (h d) -> p h d", h=8)
        t2v = t2[:, :].rearrange("p (h d) -> p h d", h=8)
        t3v = t3[:, :].rearrange("p (h d) -> p h d", h=8)
        ctx.op("vector", lambda e: e.tensor_tensor(
            t2v[:, :, 0:32], t1v[:, :, 32:64], B_b[:, :, 0:32], ALU.mult),
            inc=("vec", 1), waits=[("vec", ctx.val("vec"))])
        ctx.op("vector", lambda e: e.tensor_tensor(
            t2v[:, :, 32:64], t1v[:, :, 0:32], B_b[:, :, 32:64], ALU.mult),
            inc=("vec", 1))
        ctx.op("vector", lambda e: e.tensor_tensor(t3v, t1v, A_b, ALU.mult),
               inc=("vec", 1))
        ctx.op("vector", lambda e: e.tensor_add(tok[:, 0:TT], t3[:, :], t2[:, :]),
               inc=("vec", 1))

    rotld = None
    for tt in range(16):
        if PARANOID >= 1:
            ctx.syncpoint()
        t0 = tt * 128
        slot = tt % ring
        if kind == "l0":
            for j_ in range(4):
                rotld = ctx.dma("gpsimd", bufs["rott"][:, j_, :],
                                W["rot"][j_, t0:t0 + 128, :],
                                waits=[("vec", ctx.val("vec"))] if j_ == 0 else ())
        roles = (("q", 0, 1), ("k", 1, 2), ("v", 2, 3)) if kind == "l0" else \
                (("q", 0, 2), ("k", 2, 4), ("v", 4, 6))
        for role, b0, b1 in roles:
            for blk in range(b0, b1):
                for dk in range(NCH):
                    w = []
                    if dk == 0:
                        w.append(("vec", ctx.val("vec")))
                        w.append(("act", ctx.val("act")))
                        if tt == 0 and blk == b0 and role == "q":
                            w.append(("wdma", ldw))
                    coff = 0 if (cfg["shift"] and dk >= 4) else 1
                    ctx.op("tensor", lambda e, dk=dk, blk=blk, t0=t0, coff=coff: e.matmul(
                        ps_qkv[:, :],
                        src[:, dk, coff + t0:coff + t0 + 128],
                        wqkv[:, dk, blk * TT:(blk + 1) * TT],
                        start=(dk == 0), stop=(dk == NCH - 1)),
                        inc=("mm", 1) if dk == NCH - 1 else None, waits=w)
                mmv = ctx.val("mm")
                if (DEBUG_ATTN and b == 0 and tt == DEBUG_ATTN_TT
                        and role == "v" and blk == b0):
                    ctx.op("vector", lambda e: e.tensor_copy(Ssb[:, 0:TT], ps_qkv[:, :]),
                           inc=("vec", 1), waits=[("mm", mmv)])
                    ctx.wait("gpsimd", "vec", ctx.val("vec"))
                    ctx.dma("gpsimd", W_dbg[1][:, 7936:7936 + 256], Ssb[:, 0:256])
                    ctx.wait("vector", "gdma", ctx.val("gdma"))
                    ctx.wait("tensor", "gdma", ctx.val("gdma"))
                colo = (blk - b0) * TT
                if role == "v":
                    if kind == "l1":
                        ctx.op("vector", lambda e, colo=colo, slot=slot, blk=blk: e.tensor_tensor(
                            vring[:, slot, colo:colo + TT], ps_qkv[:, :],
                            bufs["bb"][:, blk * TT:(blk + 1) * TT], ALU.add),
                            inc=("vec", 1), waits=[("mm", mmv)])
                    else:
                        ctx.op("vector", lambda e, colo=colo, slot=slot: e.tensor_copy(
                            vring[:, slot, colo:colo + TT], ps_qkv[:, :]),
                            inc=("vec", 1), waits=[("mm", mmv)])
                else:
                    if kind == "l1":
                        ctx.op("vector", lambda e, colo=colo, blk=blk: e.tensor_tensor(
                            tok[:, colo:colo + TT], ps_qkv[:, :],
                            bufs["bb"][:, blk * TT:(blk + 1) * TT], ALU.add),
                            inc=("vec", 1), waits=[("mm", mmv)])
                    else:
                        ctx.wait("scalar", "gdma", rotld)
                        ctx.wait("vector", "gdma", rotld)
                        l0_prep(role, tt)
            if role == "v" and DEBUG_ATTN and b == 0 and tt == DEBUG_ATTN_TT:
                ctx.wait("gpsimd", "vec", ctx.val("vec"))
                ctx.dma("gpsimd", W_dbg[1][:, 7872:7872 + 64],
                        vring[:, tt % ring, 0:64])
                for c_ in range(8):
                    ctx.dma("gpsimd", W_dbg[1][:, 4096 + c_ * 1024:4096 + (c_ + 1) * 1024],
                            wqkv[:, c_, 2048:3072])
                ctx.dma("gpsimd", W_dbg[1][:, 2048:2048 + 1024],
                        W["l1a_wqkv"][384:512, 2048:3072])
            if role == "v" and DEBUG_ATTN and b == 0 and tt == 0:
                ctx.wait("gpsimd", "vec", ctx.val("vec"))
                ctx.dma("gpsimd", W_dbg[1][:, 7936:7936 + 64],
                        vring[:, 0, 0:64])
                ctx.wait("vector", "gdma", ctx.val("gdma"))
                ctx.wait("tensor", "gdma", ctx.val("gdma"))
                ctx.wait("scalar", "gdma", ctx.val("gdma"))
            if DEBUG_NOTR and role != "v":
                pass
            elif role != "v":
                dstb = qtt if role == "q" else None
                for g0 in range(0, ncp, 4):
                    gn = min(4, ncp - g0)
                    for j in range(gn):
                        cp = g0 + j
                        ctx.op("tensor", lambda e, cp=cp, j=j: e.transpose(
                            ps_tr[:, j * 128:(j + 1) * 128],
                            tok[:, cp * 128:(cp + 1) * 128], identb[:, :]),
                            inc=("mm", 1) if j == gn - 1 else None,
                            waits=[("vec", ctx.val("vec"))] if j == 0 else ())
                    mmv = ctx.val("mm")
                    if role == "q":
                        ctx.op("vector", lambda e, g0=g0, gn=gn: e.tensor_copy(
                            qtt[:, g0:g0 + gn, :],
                            ps_tr[:, 0:gn * 128].rearrange("p (g n) -> p g n", n=128)),
                            inc=("vec", 1), waits=[("mm", mmv)])
                    else:
                        ctx.op("vector", lambda e, g0=g0, gn=gn, slot=slot: e.tensor_copy(
                            kring[:, slot, g0:g0 + gn, :],
                            ps_tr[:, 0:gn * 128].rearrange("p (g n) -> p g n", n=128)),
                            inc=("vec", 1), waits=[("mm", mmv)])
        if DEBUG_ATTN and b == 0 and tt == DEBUG_ATTN_TT:
            ctx.wait("gpsimd", "vec", ctx.val("vec"))
            ctx.dma("gpsimd", W_dbg[1][:, 0:nh * 64], tok[:, :])
            ctx.dma("gpsimd", W_dbg[1][:, 1024:1024 + ncp * 128],
                    qtt[:, :, :].rearrange("p c n -> p (c n)"))
        if DEBUG_NOATTN:
            continue
        if PARANOID >= 2:
            ctx.syncpoint()
        kts = cfg["slots"](tt)
        strip = strips[:, cls_of[tt] * 640:(cls_of[tt] + 1) * 640]
        for cp in range(ncp):
            if PARANOID >= 3:
                ctx.syncpoint()
            for h2, psS in ((0, ps_sA), (1, ps_sB)):
                base = 64 * h2
                for s, kt in enumerate(kts):
                    ks = max(kt, 0) % ring
                    w = []
                    if s == 0:
                        w.append(("vec", ctx.val("vec")))
                    ctx.op("tensor", lambda e, s=s, ks=ks, cp=cp, base=base, psS=psS: e.matmul(
                        psS[:, s * 128:(s + 1) * 128],
                        qtt[base:base + 64, cp, :],
                        kring[base:base + 64, ks, cp, :],
                        start=True, stop=True,
                        tile_position=(base, 0)),
                        inc=("mm", 1) if s == 4 else None, waits=w)
            for h2, psS in ((0, ps_sA), (1, ps_sB)):
                base = 64 * h2
                h = 2 * cp + h2
                mmv = ctx.val("mm")
                ctx.op("vector", lambda e, psS=psS: e.tensor_tensor(
                    Ssb[:, :], psS[:, :], strip, ALU.add),
                    inc=("vec", 1),
                    waits=[("mm", mmv), ("act", ctx.val("act"))])
                ctx.op("vector", lambda e: e.reduce_max(
                    arow[:, 0:1], Ssb[:, :], axis=AX.X, negate=True),
                    inc=("vec", 1), waits=[("vec", ctx.val("vec"))])
                nsv = ctx.op("vector", lambda e: e.tensor_scalar_mul(
                    arow[:, 1:2], arow[:, 0:1], escale),
                    inc=("vec", 1), waits=[("vec", ctx.val("vec"))])
                ex = ctx.op("scalar", lambda e: e.activation(
                    Asb[:, :], Ssb[:, :], AF.Exp, bias=arow[:, 1:2], scale=escale,
                    accum_out=arow[:, 2:3]),
                    inc=("act", 1),
                    waits=[("vec", nsv), ("mm", ctx.val("mm"))])
                ctx.op("vector", lambda e: e.reciprocal(arow[:, 3:4], arow[:, 2:3]),
                       inc=("vec", 1), waits=[("act", ex)])
                av = ctx.op("vector", lambda e: e.tensor_scalar_mul(
                    Asb[:, :], Asb[:, :], arow[:, 3:4]),
                    inc=("vec", 1), waits=[("vec", ctx.val("vec"))])
                if DEBUG_ATTN and b == 0 and tt == DEBUG_ATTN_TT and cp == 0 and h2 == 0:
                    ctx.wait("gpsimd", "vec", av)
                    ctx.dma("gpsimd", W_dbg[1][:, 2048:2048 + 640], Asb[:, :])
                    ctx.dma("gpsimd", W_dbg[1][:, 2688:2688 + 640], Ssb[:, :])
                    ctx.wait("vector", "gdma", ctx.val("gdma"))
                for s in range(5):
                    ctx.op("tensor", lambda e, s=s: e.transpose(
                        ps_tr[:, s * 128:(s + 1) * 128],
                        Asb[:, s * 128:(s + 1) * 128], identb[:, :]),
                        inc=("mm", 1) if s == 4 else None,
                        waits=[("vec", av)] if s == 0 else ())
                ctx.op("vector", lambda e: e.tensor_copy(ATsb[:, :], ps_tr[:, :]),
                       inc=("vec", 1), waits=[("mm", ctx.val("mm"))])
                atv = ctx.val("vec")
                if DEBUG_ATTN and b == 0 and tt == DEBUG_ATTN_TT and cp == 0 and h2 == 0:
                    ctx.wait("gpsimd", "vec", atv)
                    ctx.dma("gpsimd", W_dbg[1][:, 7168:7168 + 640], ATsb[:, :])
                    ks0 = max(cfg["slots"](tt)[4], 0) % ring
                    ctx.dma("gpsimd", W_dbg[1][:, 7808:7808 + 64],
                            vring[:, ks0, 0:64])
                    ctx.wait("vector", "gdma", ctx.val("gdma"))
                    ctx.wait("tensor", "gdma", ctx.val("gdma"))
                for s, kt in enumerate(kts):
                    ks = max(kt, 0) % ring
                    ctx.op("tensor", lambda e, s=s, ks=ks, h=h, base=base: e.matmul(
                        ps_o[base:base + 64, :],
                        vring[:, ks, h * 64:(h + 1) * 64],
                        ATsb[:, s * 128:(s + 1) * 128],
                        start=(s == 0), stop=(s == 4),
                        tile_position=(0, base)),
                        inc=("mm", 1) if s == 4 else None,
                        waits=[("vec", atv)] if s == 0 else ())
            ctx.op("vector", lambda e, cp=cp, tt=tt: e.tensor_copy(
                OT[:, cp, (tt % 4) * 128:(tt % 4) * 128 + 128], ps_o[:, :]),
                inc=("vec", 1), waits=[("mm", ctx.val("mm"))])
        if tt % 4 == 3:
            g = tt // 4
            if DEBUG_ATTN and b == 0 and g == DEBUG_ATTN_TT // 4:
                ctx.wait("gpsimd", "vec", ctx.val("vec"))
                ctx.dma("gpsimd", W_dbg[1][:, 2048 + 1024:2048 + 1024 + ncp * TT],
                        OT[:, :, :].rearrange("p c n -> p (c n)"))
                ctx.wait("tensor", "gdma", ctx.val("gdma"))
            gt0 = g * TT
            if kind == "l1":
                prxf = bufs["prxf"]
                srcv = src3[b].rearrange("(c p) t -> p c t", p=128)
                dstv = dst3[b].rearrange("(c p) t -> p c t", p=128)
                for c_ in range(NCH):
                    ld = ctx.dma("scalar", prxf[:, c_, :], srcv[:, c_, gt0:gt0 + TT],
                                 waits=([("gdma", ctx.val("gdma"))] if c_ == 0
                                        else ()))
                for nchk in range(NCH):
                    for icp in range(ncp):
                        ctx.op("tensor", lambda e, nchk=nchk, icp=icp: e.matmul(
                            ps_pr[:, :], wpr[:, icp, nchk * 128:(nchk + 1) * 128],
                            OT[:, icp, :],
                            start=(icp == 0), stop=(icp == ncp - 1)),
                            inc=("mm", 1) if icp == ncp - 1 else None,
                            waits=[("vec", ctx.val("vec"))] if icp == 0 else ())
                    ctx.op("vector", lambda e, nchk=nchk: e.scalar_tensor_tensor(
                        prxf[:, nchk, :], ps_pr[:, :],
                        smalls[:, 24 + nchk:25 + nchk], prxf[:, nchk, :],
                        ALU.add, ALU.add),
                        inc=("vec", 1),
                        waits=[("mm", ctx.val("mm")), ("xdma", ld)])
                sv_ = ctx.val("vec")
                for c_ in range(NCH):
                    ctx.dma("gpsimd", dstv[:, c_, gt0:gt0 + TT], prxf[:, c_, :],
                            waits=[("vec", sv_)] if c_ == 0 else ())
            else:
                y = bufs["y"]
                for nchk in range(NCH):
                    for icp in range(ncp):
                        ctx.op("tensor", lambda e, nchk=nchk, icp=icp: e.matmul(
                            ps_pr[:, :], wpr[:, icp, nchk * 128:(nchk + 1) * 128],
                            OT[:, icp, :],
                            start=(icp == 0), stop=(icp == ncp - 1)),
                            inc=("mm", 1) if icp == ncp - 1 else None,
                            waits=[("vec", ctx.val("vec"))] if icp == 0 else ())
                    ctx.op("vector", lambda e, nchk=nchk, gt0=gt0: e.tensor_add(
                        y[:, nchk, 1 + gt0:1 + gt0 + TT],
                        y[:, nchk, 1 + gt0:1 + gt0 + TT], ps_pr[:, :]),
                        inc=("vec", 1), waits=[("mm", ctx.val("mm"))])
    if kind == "l1":
        ctx.wait("gpsimd", "gdma", ctx.val("gdma"))


def emit_ybf_refresh(ctx, bufs):
    y, ybf = bufs["y"], bufs["ybf"]
    for c in range(NCH):
        ctx.op("scalar", lambda e, c=c: e.copy(ybf[:, c, :], y[:, c, :]),
               inc=("act", 1))
    ctx.wait("scalar", "act", ctx.val("act"))



QUARTERS = [(0, 6), (6, 12), (12, 17), (17, 22)]


@contextlib.contextmanager
def l0f_stats_scratch(nc):
    with (
        SB(nc, "sb_fxf", [128, NCH, TT], F32) as fxf,
        SB(nc, "sb_fcast", [128, NCH, TT], BF16) as cast,
        SB(nc, "sb_fsq", [128, NCH, TT], BF16) as xq,
        SB(nc, "sb_ln_rows", [1, 4 * TT], F32) as rows,
        SB(nc, "sb_ln_rbf", [1, 2 * TT], BF16) as rbf,
        SB(nc, "sb_stmp", [128, TT], F32) as stmp,
        PS(nc, "ps__a", [128, TT], F32) as ps_a,
        PS(nc, "ps__b", [128, TT], F32) as ps_b,
        PS(nc, "ps__c", [128, TT], F32) as ps_c,
        PS(nc, "ps__d", [128, TT], F32) as ps_d,
    ):
        yield {"fxf": fxf, "fcast": cast, "fsq": xq, "ln_rows": rows,
               "ln_rbf": rbf, "stmp": stmp,
               "ps_a": ps_a, "ps_b": ps_b, "ps_c": ps_c, "ps_d": ps_d}


@contextlib.contextmanager
def l0f_ffn_scratch(nc):
    with (
        SB(nc, "sb_w1q", [128, NCH, 2 * 6 * 128], BF16) as w1q,
        SB(nc, "sb_w2q", [128, 6, D], BF16) as w2q,
        SB(nc, "sb_h2glu", [128, 6, TT], BF16) as h2glu,
        SB(nc, "sb_aa", [128, TT], F32) as aa,
        SB(nc, "sb_gg", [128, TT], F32) as gg,
        PS(nc, "ps__fa", [128, TT], F32) as ps_fa,
        PS(nc, "ps__fg", [128, TT], F32) as ps_fg,
        PS(nc, "ps__fy0", [128, TT], F32) as ps_fy0,
        PS(nc, "ps__fy1", [128, TT], F32) as ps_fy1,
    ):
        yield {"w1q": w1q, "w2q": w2q, "h2glu": h2glu,
               "aa": aa, "gg": gg, "ps_fa": ps_fa, "ps_fg": ps_fg,
               "ps_fy": [ps_fy0, ps_fy1]}


def emit_l0f_stats(ctx, bufs, l):
    y = bufs["y"]
    h2n = bufs["h2n"]
    cast, xq = bufs["fcast"], bufs["fsq"]
    rows, rbf = bufs["ln_rows"], bufs["ln_rbf"]
    ps_s, ps_q, ps_r, ps_m = bufs["ps_a"], bufs["ps_b"], bufs["ps_c"], bufs["ps_d"]
    ones_col, ones_row = bufs["ones_colb"], bufs["ones_rowb"]
    smalls = bufs["smalls"]
    SUM, VAR, RSTD, MR = 0, TT, 2 * TT, 3 * TT

    def shifted(c0, c1, t0, wdt):
        off = 1 if c0 < 4 else 0
        return y[:, c0:c1, off + t0:off + t0 + wdt]

    for ti in range(NTT):
        t0 = ti * TT
        cv = ctx.val("vec")
        av = ctx.val("act")
        fxf = bufs["fxf"]
        ld = None
        for c_ in range(NCH):
            off = 1 if c_ < 4 else 0
            ld = ctx.dma("gpsimd", fxf[:, c_, :], y[:, c_, off + t0:off + t0 + TT],
                         waits=([("vec", cv), ("act", av)] if c_ == 0 else ()))
        c1 = ctx.op("vector", lambda e: e.tensor_copy(cast[:, :, :], fxf[:, :, :]),
                    inc=("vec", 1), waits=[("gdma", ld)])
        c1 = ("vec", c1)
        sqv = ctx.op("scalar", lambda e: e.activation(xq[:, :, :], cast[:, :, :], AF.Square),
                     inc=("act", 1), waits=[c1])
        for c in range(NCH):
            ctx.op("tensor", lambda e, c=c: e.matmul(
                ps_s[0:1, :], ones_col[:, :], cast[:, c, :],
                start=(c == 0), stop=(c == NCH - 1)),
                inc=("mm", 1) if c == NCH - 1 else None,
                waits=[c1] if c == 0 else ())
        for c in range(NCH):
            ctx.op("tensor", lambda e, c=c: e.matmul(
                ps_q[0:1, :], ones_col[:, :], xq[:, c, :],
                start=(c == 0), stop=(c == NCH - 1)),
                inc=("mm", 1) if c == NCH - 1 else None,
                waits=[("act", sqv)] if c == 0 else ())
        mmv = ctx.val("mm")
        ctx.op("vector", lambda e: e.tensor_scalar_mul(
            rows[0:1, SUM:SUM + TT], ps_s[0:1, :], 1.0 / D),
            inc=("vec", 1), waits=[("mm", mmv)])
        ctx.op("vector", lambda e: e.tensor_mul(
            rows[0:1, VAR:VAR + TT], rows[0:1, SUM:SUM + TT], rows[0:1, SUM:SUM + TT]),
            inc=("vec", 1))
        ctx.op("vector", lambda e: e.scalar_tensor_tensor(
            rows[0:1, VAR:VAR + TT], ps_q[0:1, :], 1.0 / D, rows[0:1, VAR:VAR + TT],
            ALU.mult, ALU.subtract),
            inc=("vec", 1))
        sq2 = ctx.op("scalar", lambda e: e.activation(
            rows[0:1, RSTD:RSTD + TT], rows[0:1, VAR:VAR + TT], AF.Sqrt,
            bias=bufs["misc"][0:1, 0:1]),
            inc=("act", 1), waits=[("vec", ctx.val("vec"))])
        ctx.op("vector", lambda e: e.reciprocal(
            rows[0:1, RSTD:RSTD + TT], rows[0:1, RSTD:RSTD + TT]),
            inc=("vec", 1), waits=[("act", sq2)])
        ctx.op("vector", lambda e: e.tensor_mul(
            rows[0:1, MR:MR + TT], rows[0:1, SUM:SUM + TT], rows[0:1, RSTD:RSTD + TT]),
            inc=("vec", 1))
        cbf = ctx.op("vector", lambda e: e.tensor_copy(
            rbf[0:1, :], rows[0:1, RSTD:RSTD + 2 * TT]),
            inc=("vec", 1))
        ctx.op("tensor", lambda e: e.matmul(ps_r[:, :], ones_row[:, :], rbf[0:1, 0:TT],
                                            start=True, stop=True),
               inc=None, waits=[("vec", cbf)])
        bc = ctx.op("tensor", lambda e: e.matmul(ps_m[:, :], ones_row[:, :],
                                                 rbf[0:1, TT:2 * TT],
                                                 start=True, stop=True),
                    inc=("mm", 1))
        stmp = bufs["stmp"]
        for c in reversed(range(NCH)):
            ctx.op("vector", lambda e, c=c: e.tensor_tensor(
                stmp[:, :], bufs["fxf"][:, c, :], ps_r[:, :], ALU.mult),
                inc=("vec", 1),
                waits=[("mm", bc)] if c == NCH - 1 else [("vec", ctx.val("vec"))])
            ctx.op("vector", lambda e, c=c: e.tensor_tensor(
                stmp[:, :], stmp[:, :], ps_m[:, :], ALU.subtract),
                inc=("vec", 1), waits=[("vec", ctx.val("vec"))])
            ctx.op("vector", lambda e, c=c, t0=t0, l=l: e.tensor_scalar(
                h2n[c][:, t0:t0 + TT], stmp[:, :],
                smalls[:, 32 + 8 * l + c:33 + 8 * l + c],
                smalls[:, 48 + 8 * l + c:49 + 8 * l + c],
                ALU.mult, ALU.add),
                inc=("vec", 1), waits=[("vec", ctx.val("vec"))])
        if PARANOID >= 1:
            ctx.syncpoint()


def emit_l0f_ffn(ctx, bufs, W, l):
    y, h2n = bufs["y"], bufs["h2n"]
    w1q, w2q, h2glu = bufs["w1q"], bufs["w2q"], bufs["h2glu"]
    aa, gg = bufs["aa"], bufs["gg"]
    ps_fa, ps_fg, ps_fy = bufs["ps_fa"], bufs["ps_fg"], bufs["ps_fy"]
    smalls = bufs["smalls"]
    w1src = W["l0f_w1"][l]
    w2src = W["l0f_w2"][l]
    for qi, (hc0, hc1) in enumerate(QUARTERS):
        nq = hc1 - hc0
        wmm = [("mm", ctx.val("mm"))]
        l1d = ctx.dma_chunks("sync", w1q[:, :, 0:nq * 128],
                             w1src[:, hc0 * 128:hc1 * 128], NCH, waits=wmm)
        l1d = ctx.dma_chunks("sync", w1q[:, :, 6 * 128:6 * 128 + nq * 128],
                             w1src[:, 2816 + hc0 * 128:2816 + hc1 * 128], NCH)
        l1d = ctx.dma_chunks("sync", w2q[:, 0:nq, :],
                             w2src[hc0 * 128:hc1 * 128, :], nq)
        for st in range(NTT):
            t0 = st * TT
            for j in range(nq):
                hc = hc0 + j
                for dk in range(NCH):
                    w = []
                    if dk == 0:
                        w.append(("vec", ctx.val("vec")))
                        if j == 0 and st == 0:
                            w.append(("wdma", l1d))
                    ctx.op("tensor", lambda e, dk=dk, j=j, t0=t0: e.matmul(
                        ps_fa[:, :], w1q[:, dk, j * 128:(j + 1) * 128],
                        h2n[:, dk, 1 + t0:1 + t0 + TT],
                        start=(dk == 0), stop=(dk == NCH - 1)),
                        inc=("mm", 1) if dk == NCH - 1 else None, waits=w)
                for dk in range(NCH):
                    w = [("act", ctx.val("act"))] if dk == 0 else ()
                    ctx.op("tensor", lambda e, dk=dk, j=j, t0=t0: e.matmul(
                        ps_fg[:, :], w1q[:, dk, 6 * 128 + j * 128:6 * 128 + (j + 1) * 128],
                        h2n[:, dk, 1 + t0:1 + t0 + TT],
                        start=(dk == 0), stop=(dk == NCH - 1)),
                        inc=("mm", 1) if dk == NCH - 1 else None, waits=w)
                mmv = ctx.val("mm")
                gv = ctx.op("scalar", lambda e, hc=hc, l=l: e.activation(
                    gg[:, :], ps_fg[:, :], AF.Gelu,
                    bias=smalls[:, 64 + 44 * l + 22 + hc:65 + 44 * l + 22 + hc]),
                    inc=("act", 1),
                    waits=[("mm", mmv), ("vec", ctx.val("vec"))])
                ctx.op("vector", lambda e, hc=hc, l=l: e.tensor_scalar_add(
                    aa[:, :], ps_fa[:, :],
                    smalls[:, 64 + 44 * l + hc:65 + 44 * l + hc]),
                    inc=("vec", 1), waits=[("mm", mmv)])
                ctx.op("vector", lambda e, j=j: e.tensor_mul(
                    h2glu[:, j, :], aa[:, :], gg[:, :]),
                    inc=("vec", 1), waits=[("act", gv)])
            hv = ctx.val("vec")
            for dc in range(NCH):
                pb = ps_fy[dc % 2]
                for j in range(nq):
                    w = []
                    if j == 0:
                        w.append(("vec", hv if dc < 2 else ctx.val("vec")))
                    ctx.op("tensor", lambda e, dc=dc, j=j, pb=pb: e.matmul(
                        pb[:, :], w2q[:, j, dc * 128:(dc + 1) * 128], h2glu[:, j, :],
                        start=(j == 0), stop=(j == nq - 1)),
                        inc=("mm", 1) if j == nq - 1 else None, waits=w)
                if qi == 0:
                    ctx.op("vector", lambda e, dc=dc, t0=t0, pb=pb, l=l: e.scalar_tensor_tensor(
                        y[:, dc, 1 + t0:1 + t0 + TT], pb[:, :],
                        smalls[:, 152 + 8 * l + dc:153 + 8 * l + dc],
                        y[:, dc, 1 + t0:1 + t0 + TT], ALU.add, ALU.add),
                        inc=("vec", 1), waits=[("mm", ctx.val("mm"))])
                else:
                    ctx.op("vector", lambda e, dc=dc, t0=t0, pb=pb: e.tensor_add(
                        y[:, dc, 1 + t0:1 + t0 + TT],
                        y[:, dc, 1 + t0:1 + t0 + TT], pb[:, :]),
                        inc=("vec", 1), waits=[("mm", ctx.val("mm"))])



def prep_host_inputs(inputs):
    f = np.ascontiguousarray
    HID = 2730
    HIDP = 2816
    c = {}
    c["f32_l0a_wq"] = f(inputs["L0_attn_wq"].T)
    c["f32_l0a_wk"] = f(inputs["L0_attn_wk"].T)
    c["f32_l0a_wv"] = f(inputs["L0_attn_wv"].T)
    c["f32_l0a_wo"] = f(inputs["L0_attn_wo"].T)
    c["f32_l0f_wqkv"] = f(inputs["L0_ffn_wqkv"])
    c["f32_l0f_wo"] = f(inputs["L0_ffn_wo"])
    w1 = inputs["L0_ffn_w1"]
    w1p = np.zeros((2, D, 2 * HIDP), dtype=np.float32)
    w1p[:, :, :HID] = w1[:, :, :HID]
    w1p[:, :, HIDP:HIDP + HID] = w1[:, :, HID:]
    c["f32_l0f_w1"] = w1p
    w2 = inputs["L0_ffn_w2"]
    w2p = np.zeros((2, HIDP, D), dtype=np.float32)
    w2p[:, :HID, :] = w2
    c["f32_l0f_w2"] = w2p
    c["f32_l1a_wqkv"] = f(inputs["L1_attn_W_qkv"])
    c["f32_l1a_wproj"] = f(inputs["L1_attn_W_proj"])
    c["f32_moe_gw"] = f(inputs["L1_ffn_gate_w"].T)
    c["f32_moe_wg"] = f(inputs["L1_ffn_w_gate"].transpose(0, 2, 1))
    c["f32_moe_wu"] = f(inputs["L1_ffn_w_up"].transpose(0, 2, 1))
    c["f32_moe_wd"] = f(inputs["L1_ffn_w_down"].transpose(0, 2, 1))
    c["f32_moe_sg"] = f(inputs["L1_ffn_shared_gate_w"].T)
    c["f32_moe_su"] = f(inputs["L1_ffn_shared_up_w"].T)
    c["f32_moe_sd"] = f(inputs["L1_ffn_shared_down_w"].T)
    def colmajor(v):
        v = np.asarray(v, dtype=np.float32)
        n = v.size // 128
        return f(v.reshape(n, 128).T)
    c["b_qkv"] = colmajor(inputs["L1_attn_b_qkv"])
    c["b_proj"] = colmajor(inputs["L1_attn_b_proj"])
    c["ln_g"] = np.stack([colmajor(inputs["L0_ffn_ln_g"][l]) for l in range(2)])
    c["ln_b"] = np.stack([colmajor(inputs["L0_ffn_ln_b"][l]) for l in range(2)])
    b1 = inputs["L0_ffn_b1"]
    b1p = np.zeros((2, 2 * HIDP), dtype=np.float32)
    b1p[:, :HID] = b1[:, :HID]
    b1p[:, HIDP:HIDP + HID] = b1[:, HID:]
    c["b1"] = np.stack([colmajor(b1p[l]) for l in range(2)])
    c["b2"] = np.stack([colmajor(inputs["L0_ffn_b2"][l]) for l in range(2)])
    qs = np.tile(inputs["L0_ffn_q_scale"], (1, 8))
    ks = np.tile(inputs["L0_ffn_k_scale"], (1, 8))
    c["qsks_row"] = f(np.concatenate(
        [qs[0], ks[0], qs[1], ks[1]])[None, :].astype(np.float32))
    c["b_qkv_row"] = f(np.asarray(inputs["L1_attn_b_qkv"], np.float32)[None, :])
    c["ebias_row"] = f(np.asarray(inputs["L1_ffn_expert_bias"], np.float32)[None, :])
    c["rot"] = build_rotary_tables()
    c["misc_row"] = np.array([[1e-5, 0, 0, 0, 0, 0, 0, 0]], np.float32)
    c["ones_colf"] = np.ones((128, 1), np.float32)
    c["ones_rowf"] = np.ones((1, 128), np.float32)
    c["identf"] = np.eye(128, dtype=np.float32)
    c["ones_colb"] = np.ones((128, 1), ml_dtypes.bfloat16)
    c["ones_rowb"] = np.ones((1, 128), ml_dtypes.bfloat16)
    c["identb"] = np.eye(128, dtype=ml_dtypes.bfloat16)
    l0s, l0cls = build_strips(l0_local_mask_np(T), l0_slots)
    l1s, l1cls = build_strips(fenwick_mask_np(T), l1_slots)
    assert l0s.shape[0] <= 5 and l1s.shape[0] <= 5
    pad = lambda a: np.concatenate(
        [a, np.zeros((5 - a.shape[0], 128, 640), np.float32)], 0) if a.shape[0] < 5 else a
    c["l0_strips"] = pad(l0s).astype(ml_dtypes.bfloat16)
    c["l1_strips"] = pad(l1s).astype(ml_dtypes.bfloat16)
    c["_l0cls"] = l0cls
    c["_l1cls"] = l1cls

    x = np.asarray(inputs["L0_attn_x"], np.float32)
    x_fm = np.ascontiguousarray(x.transpose(0, 2, 1))
    shards = [x_fm[i * B_PER_CORE:(i + 1) * B_PER_CORE] for i in range(NCORES)]
    return c, shards


def kernel(**inputs):
    common, shards = prep_host_inputs(inputs)
    common.pop("_l0cls")
    common.pop("_l1cls")
    nc = build_program()
    in_maps = [dict(common, x_in=s) for s in shards]
    res = run_bass_kernel_spmd(nc, in_maps, core_ids=list(range(NCORES)))
    out = np.concatenate([r["y_out"] for r in res.results], axis=0)
    return np.ascontiguousarray(out.transpose(0, 2, 1))
